# version 10
# baseline (speedup 1.0000x reference)
import numpy as np
import ml_dtypes
import concourse.bass as bass
import concourse.mybir as mybir
from concourse.bass_utils import run_bass_kernel_spmd
from contextlib import ExitStack

F32 = mybir.dt.float32
BF16 = mybir.dt.bfloat16
U32 = mybir.dt.uint32
AF = mybir.ActivationFunctionType
ALU = mybir.AluOpType
AX = mybir.AxisListType

B, S, D = 4, 4096, 1024
H = 16
SH = S // 2
F = 4096
E, R = 8, 4
ER = E * R
NP = 8
DC = 8
TW = 3968
TC0 = 1920
ARENA = 194560
CH = ["t", "v", "s", "g"]
DM = ["dw", "dwi", "dwq0", "dwq1", "dta0", "dta1", "dtb0", "dtb1",
      "dst0", "dst1", "dbc", "dsp0", "dsp1", "dsv0", "dsv1",
      "dff0", "dff1", "dx0", "dx1", "do0", "do1"]


def build_program():
    nc = bass.Bass("TRN2", target_bir_lowering=False)

    def din(name, shape, dtype=F32):
        return nc.dram_tensor(name, shape, dtype, kind="ExternalInput").ap()

    x_in = din("x", [S, D])
    wqkv_sb = din("wqkv_sb", [NP, 128, 3072])
    bq_sb = din("bq_sb", [128, DC])
    bk_sb = din("bk_sb", [128, DC])
    bv_row = din("bv_row", [1, D])
    wout_sb = din("wout_sb", [DC, 128, D])
    boutbc_in = din("boutbc", [128, D])
    wi_sb = din("wi_sb", [32, 128, D])
    wib_sb = din("wib_sb", [128, 32])
    wo_sb = din("wo_sb", [F, D])
    wobbc_in = din("wobbc", [128, D])
    gw_sb = din("gw_sb", [128, DC * E])
    gb_sb = din("gb_sb", [E, 1])
    la_sb = din("la_sb", [128, DC * ER])
    lb_sb = din("lb_sb", [ER, F])
    t1_in = din("T1", [H, 128, TW])
    t2_in = din("T2", [H, 128, TW])
    idbf_in = din("idbf", [128, 128], BF16)
    idf_in = din("idf32", [128, 128], F32)
    ec_in = din("ecols", [128, ER], F32)

    out = nc.dram_tensor("out", [SH, D], F32, kind="ExternalOutput").ap()

    kT_sp = nc.dram_tensor("kT_sp", [NP, 128, S], BF16).ap()
    v_sp = nc.dram_tensor("v_sp", [NP, 128, 2 * 4096], BF16).ap()
    qT_sp = nc.dram_tensor("qT_sp", [NP, 128, SH], BF16).ap()
    ffn0_sp = nc.dram_tensor("ffn0_sp", [SH, D], F32).ap()
    ffn1_sp = nc.dram_tensor("ffn1_sp", [SH, D], F32).ap()
    aty_sp = nc.dram_tensor("aty_sp", [SH, D], F32).ap()
    bc_scr = nc.dram_tensor("bc_scr", [1, 1024], F32).ap()

    marks = {}

    def prog(cx):
        emit, cnt, eng, sem, V, PV = (cx["emit"], cx["cnt"], cx["eng"],
                                      cx["sem"], cx["V"], cx["PV"])

        def op(e, f, *a, **kw):
            inc = kw.pop("_inc", 1)
            if emit:
                if cnt[e] > 0 and e != "t":
                    eng[e].wait_ge(sem[e], cnt[e])
                r = getattr(eng[e], f)(*a, **kw)
                if inc:
                    r.then_inc(sem[e], inc)
            cnt[e] += 1

        def dma(e, ds, outap, inap):
            if emit:
                eng[e].dma_start(out=outap, in_=inap).then_inc(sem[ds], 16)
            cnt[ds] += 16

        def mark(key, tgt):
            if not emit:
                marks[key] = (tgt, cnt[tgt])

        def wt(e, key):
            if emit:
                tgt, val = marks[key]
                if val > 0:
                    eng[e].wait_ge(sem[tgt], val)

        def barrier(name):
            if not emit:
                for t in CH:
                    marks[f"bar_{name}_{t}"] = (t, cnt[t])
            else:
                for e in CH + ["sy"]:
                    for t in CH:
                        if t == e:
                            continue
                        tgt, val = marks[f"bar_{name}_{t}"]
                        if val > 0:
                            eng[e].wait_ge(sem[tgt], val)

        dma("sy", "dw", V("idbf"), idbf_in)
        dma("sy", "dw", V("idf32"), idf_in)
        dma("sy", "dw", V("ecols"), ec_in)
        dma("sy", "dw", V("bq_t"), bq_sb)
        dma("sy", "dw", V("bk_t"), bk_sb)
        dma("sy", "dw", V("wib_t"), wib_sb)
        dma("sy", "dw", V("gb_t"), gb_sb)
        dma("sy", "dw", V("gwf"), gw_sb)
        dma("sy", "dw", V("laf"), la_sb)
        dma("sy", "dw", V("bvf"), bv_row)
        dma("sy", "dw", V("wob_bc"), wobbc_in)
        dma("sy", "dw", V("bout_bc"), boutbc_in)
        dma("sy", "dw", V("lbf"), lb_sb)
        mark("pre_dw", "dw")
        wt("v", "pre_dw")
        op("v", "memset", V("ones1"), 1.0)
        for bi in range(2):
            op("v", "memset", V(f"vsp{bi}")[:, 64:4096:128], 1.0)
            op("v", "memset",
               V(f"vsp{bi}").rearrange("p (tc c) -> p tc c", tc=32)[:, :,
                                                                   65:128],
               0.0)
        op("v", "tensor_scalar_mul", V("bqs_t"), V("bq_t"), 0.125)
        op("v", "tensor_copy", V("bvbf"), V("bvf"))
        op("v", "tensor_copy", V("gwbf"), V("gwf"))
        op("v", "tensor_copy", V("labf"), V("laf"))
        op("v", "tensor_copy", V("lbbf"), V("lbf"))
        mark("pre_conv", "v")
        for fs in range(16):
            if emit and fs >= 1:
                tgt, val = marks[f"wi0c_{fs - 1}"]
                eng["sy"].wait_ge(sem[tgt], val)
            dma("sy", "dwi", V("wistp"), wi_sb[fs])
            mark(f"wi0d_{fs}", "dwi")
            wt("g", f"wi0d_{fs}")
            op("g", "tensor_copy", V("wih")[:, fs * 1024:(fs + 1) * 1024],
               V("wistp"))
            mark(f"wi0c_{fs}", "g")
        mark("ffnw0", "g")

        for tc in range(32):
            bi = tc % 2
            if emit and tc >= 2:
                tgt, val = marks[f"xa_free_{tc - 2}"]
                eng["sy"].wait_ge(sem[tgt], val)
            dma("sy", f"dx{bi}", V(f"xa{bi}"), x_in[tc * 128:(tc + 1) * 128, :])
            mark(f"xa_in_{tc}", f"dx{bi}")
            wt("v", f"xa_in_{tc}")
            xa = V(f"xa{bi}")
            op("v", "reduce_sum", V("st1")[:, 0:1], xa, axis=AX.X)
            op("v", "tensor_scalar_mul", V("st1")[:, 1:2], V("st1")[:, 0:1],
               1.0 / D)
            op("v", "tensor_scalar_sub", V("xc"), xa, V("st1")[:, 1:2])
            mark(f"xa_free_{tc}", "v")
            if emit:
                eng["s"].wait_ge(sem["v"], cnt["v"])
            op("s", "activation", V("sq"), V("xc"), AF.Square,
               accum_out=V("st1")[:, 2:3])
            if emit:
                eng["v"].wait_ge(sem["s"], cnt["s"])
            op("v", "tensor_scalar", V("st1")[:, 3:4], V("st1")[:, 2:3],
               1.0 / D, 1e-5, ALU.mult, ALU.add)
            if emit:
                eng["s"].wait_ge(sem["v"], cnt["v"])
            op("s", "activation", V("st1")[:, 4:5], V("st1")[:, 3:4], AF.Sqrt)
            if emit:
                eng["v"].wait_ge(sem["s"], cnt["s"])
            op("v", "reciprocal", V("st1")[:, 5:6], V("st1")[:, 4:5])
            op("v", "tensor_scalar_mul", V(f"xnc{bi}"), V("xc"),
               V("st1")[:, 5:6])
            mark(f"xnc_{tc}", "v")
            wt("t", f"xnc_{tc}")
            if emit and tc >= 2:
                tgt, val = marks[f"trcopy_{tc - 2}"]
                eng["t"].wait_ge(sem[tgt], val)
            for dc in range(DC):
                op("t", "transpose", PV(f"tr{bi}")[:, dc * 128:(dc + 1) * 128],
                   V(f"xnc{bi}")[:, dc * 128:(dc + 1) * 128], V("idbf"))
            mark(f"tr_{tc}", "t")
            wt("v", f"tr_{tc}")
            own = tc < 16
            xnT = V("xnT_own") if own else V("xnT_oth")
            col = (tc if own else tc - 16) * 128
            dst = xnT.rearrange("p (dc t) -> p dc t", dc=DC)[:, :,
                                                             col:col + 128]
            src = PV(f"tr{bi}").rearrange("p (dc t) -> p dc t", dc=DC)
            op("v", "tensor_copy", dst, src)
            mark(f"trcopy_{tc}", "v")
        barrier("A")

        def xnT_sl(dc, t0, tlen):
            if t0 < SH:
                return V("xnT_own")[:, dc * SH + t0: dc * SH + t0 + tlen]
            t0 -= SH
            return V("xnT_oth")[:, dc * SH + t0: dc * SH + t0 + tlen]

        for p in range(NP):
            bi = p % 2
            if emit and p >= 2:
                tgt, val = marks[f"wconv_{p - 2}"]
                eng["sy"].wait_ge(sem[tgt], val)
            dma("sy", f"dwq{bi}", V(f"wf32{bi}"), wqkv_sb[p])
            mark(f"wdma_{p}", f"dwq{bi}")
            wt("s", f"wdma_{p}")
            if emit and p >= 2:
                tgt, val = marks[f"wbf_free_{p - 2}"]
                eng["s"].wait_ge(sem[tgt], val)
            op("s", "mul", V(f"wbf{bi}")[:, 0:D], V(f"wf32{bi}")[:, 0:D],
               0.125)
            op("s", "copy", V(f"wbf{bi}")[:, D:3 * D],
               V(f"wf32{bi}")[:, D:3 * D])
            mark(f"wconv_{p}", "s")
            wt("t", f"wconv_{p}")
            wt("t", "pre_conv")
            for st in range(8):
                pb = st % 2
                if emit and st >= 2:
                    tgt, val = marks[f"kevac_{p}_{st - 2}"]
                    eng["t"].wait_ge(sem[tgt], val)
                for dc in range(DC):
                    op("t", "matmul", PV(f"pk{pb}"),
                       V(f"wbf{bi}")[:, D + dc * 128:D + (dc + 1) * 128],
                       xnT_sl(dc, st * 512, 512),
                       start=(dc == 0), stop=(dc == DC - 1),
                       _inc=(DC if dc == DC - 1 else 0))
                mark(f"kps_{p}_{st}", "t")
                wt("s", f"kps_{p}_{st}")
                if emit and st >= 2:
                    tgt, val = marks[f"kspill_{p}_{st - 2}"]
                    eng["s"].wait_ge(sem[tgt], val)
                op("s", "add", V(f"ksp{pb}"), PV(f"pk{pb}"),
                   V("bk_t")[:, p:p + 1])
                mark(f"kevac_{p}_{st}", "s")
                wt("g", f"kevac_{p}_{st}")
                dma("g", f"dsp{pb}", kT_sp[p, :, st * 512:(st + 1) * 512],
                    V(f"ksp{pb}"))
                mark(f"kspill_{p}_{st}", f"dsp{pb}")
            for st in range(4):
                pb = st % 2
                if emit and st >= 2:
                    tgt, val = marks[f"qevac_{p}_{st - 2}"]
                    eng["t"].wait_ge(sem[tgt], val)
                for dc in range(DC):
                    op("t", "matmul", PV(f"pq{pb}"),
                       V(f"wbf{bi}")[:, dc * 128:(dc + 1) * 128],
                       xnT_sl(dc, st * 512, 512),
                       start=(dc == 0), stop=(dc == DC - 1),
                       _inc=(DC if dc == DC - 1 else 0))
                mark(f"qps_{p}_{st}", "t")
                wt("s", f"qps_{p}_{st}")
                if emit and st >= 2:
                    tgt, val = marks[f"qspill_{p}_{st - 2}"]
                    eng["s"].wait_ge(sem[tgt], val)
                op("s", "add", V(f"qsp{pb}"), PV(f"pq{pb}"),
                   V("bqs_t")[:, p:p + 1])
                mark(f"qevac_{p}_{st}", "s")
                wt("g", f"qevac_{p}_{st}")
                dma("g", f"dsp{pb}", qT_sp[p, :, st * 512:(st + 1) * 512],
                    V(f"qsp{pb}"))
                mark(f"qspill_{p}_{st}", f"dsp{pb}")
            if emit and p >= 1:
                for side in range(2):
                    tgt, val = marks[f"vspill_{p - 1}_{side}"]
                    eng["v"].wait_ge(sem[tgt], val)
            for tc in range(32):
                pb = tc % 2
                if emit and tc >= 2:
                    tgt, val = marks[f"vevac_{p}_{tc - 2}"]
                    eng["t"].wait_ge(sem[tgt], val)
                for dc in range(DC):
                    op("t", "matmul", PV(f"pv{pb}"),
                       xnT_sl(dc, tc * 128, 128),
                       V(f"wbf{bi}")[:, 2 * D + dc * 128:2 * D + (dc + 1)
                                     * 128],
                       start=(dc == 0), stop=False, _inc=0)
                op("t", "matmul", PV(f"pv{pb}"), V("ones1"),
                   V("bvbf")[0:1, p * 128:(p + 1) * 128],
                   start=False, stop=True, _inc=DC + 1)
                mark(f"vps_{p}_{tc}", "t")
                wt("v", f"vps_{p}_{tc}")
                op("v", "tensor_copy", V("vsp0")[:, tc * 128:tc * 128 + 64],
                   PV(f"pv{pb}")[:, 0:64])
                op("v", "tensor_copy", V("vsp1")[:, tc * 128:tc * 128 + 64],
                   PV(f"pv{pb}")[:, 64:128])
                mark(f"vevac_{p}_{tc}", "v")
            mark(f"vdone_{p}", "v")
            for side in range(2):
                wt("g", f"vdone_{p}")
                dma("g", f"dsv{side}", v_sp[p, :, side * 4096:(side + 1) * 4096],
                    V(f"vsp{side}"))
                mark(f"vspill_{p}_{side}", f"dsv{side}")
            mark(f"wbf_free_{p}", "t")
        barrier("B")

        wt("t", "pre_conv")
        for qt in range(4):
            pb = qt % 2
            if emit and qt >= 2:
                tgt, val = marks[f"gevac_{qt - 2}"]
                eng["t"].wait_ge(sem[tgt], val)
            for dc in range(DC):
                op("t", "matmul", PV(f"pg{pb}")[0:E, :],
                   V("gwbf")[:, dc * E:(dc + 1) * E],
                   V("xnT_own")[:, dc * SH + qt * 512: dc * SH + qt * 512
                                + 512],
                   start=(dc == 0), stop=(dc == DC - 1))
            mark(f"gps_{qt}", "t")
            wt("s", f"gps_{qt}")
            op("s", "add", V("glog")[0:E, qt * 512:(qt + 1) * 512],
               PV(f"pg{pb}")[0:E, :], V("gb_t")[0:E, 0:1])
            mark(f"gevac_{qt}", "s")
        for tc in range(16):
            pb = tc % 2
            if emit:
                tgt, val = marks[f"gevac_{tc // 4}"]
                eng["t"].wait_ge(sem[tgt], val)
                if tc >= 2:
                    tgt, val = marks[f"amax_{tc - 2}"]
                    eng["t"].wait_ge(sem[tgt], val)
            op("t", "transpose", PV(f"pgt{pb}"),
               V("glog")[0:E, tc * 128:(tc + 1) * 128], V("idf32")[0:E, 0:E])
            mark(f"gtr_{tc}", "t")
            wt("v", f"gtr_{tc}")
            op("v", "max", V("gmx"), PV(f"pgt{pb}"))
            op("v", "max_index", V("gmi"), V("gmx"), PV(f"pgt{pb}"))
            mark(f"amax_{tc}", "v")
            op("v", "tensor_copy", V("gif"), V("gmi")[:, 0:1])
            op("v", "tensor_single_scalar", V("mtok"), V("ecols"),
               V("gif")[:, 0:1], ALU.is_equal)
            mark(f"mtok_{tc}", "v")
            wt("t", f"mtok_{tc}")
            if emit and tc >= 2:
                tgt, val = marks[f"mcp_{tc - 2}"]
                eng["t"].wait_ge(sem[tgt], val)
            op("t", "transpose", PV(f"pmt{pb}")[0:ER, :], V("mtok"),
               V("idbf"))
            mark(f"mT_{tc}", "t")
            wt("v", f"mT_{tc}")
            op("v", "tensor_copy", V("maskT")[0:ER, tc * 128:(tc + 1) * 128],
               PV(f"pmt{pb}")[0:ER, :])
            mark(f"mcp_{tc}", "v")
        for qt in range(4):
            pb = qt % 2
            if emit:
                tgt, val = marks["gevac_3"]
                eng["t"].wait_ge(sem[tgt], val)
                if qt >= 2:
                    tgt, val = marks[f"tp_{qt - 2}"]
                    eng["t"].wait_ge(sem[tgt], val)
            for dc in range(DC):
                op("t", "matmul", PV(f"pt32{pb}")[0:ER, :],
                   V("labf")[:, dc * ER:(dc + 1) * ER],
                   V("xnT_own")[:, dc * SH + qt * 512: dc * SH + qt * 512
                                + 512],
                   start=(dc == 0), stop=(dc == DC - 1))
            mark(f"tps_{qt}", "t")
            wt("v", f"tps_{qt}")
            if emit:
                tgt, val = marks["mcp_15"]
                eng["v"].wait_ge(sem[tgt], val)
            op("v", "tensor_mul", V("tp")[0:ER, qt * 512:(qt + 1) * 512],
               PV(f"pt32{pb}")[0:ER, :],
               V("maskT")[0:ER, qt * 512:(qt + 1) * 512])
            mark(f"tp_{qt}", "v")

        for sp in range(2):
            if sp == 0:
                wt("t", "ffnw0")
                if emit:
                    for t in ("dsv0", "dsv1", "dsp0", "dsp1"):
                        tgt, val = marks[f"spill_total_{t}"]
                        if val > 0:
                            eng["g"].wait_ge(sem[tgt], val)
                for fs in range(16):
                    if emit and fs >= 2:
                        tgt, val = marks[f"wo0c_{fs - 2}"]
                        eng["g"].wait_ge(sem[tgt], val)
                    dma("g", f"dtb{fs % 2}", V("wost" if fs % 2 == 0 else "wost1"),
                        wo_sb[fs * 128:(fs + 1) * 128])
                    mark(f"wo0d_{fs}", f"dtb{fs % 2}")
                    wt("v", f"wo0d_{fs}")
                    op("v", "tensor_copy",
                       V("woh")[:, fs * 1024:(fs + 1) * 1024],
                       V("wost" if fs % 2 == 0 else "wost1"))
                    mark(f"wo0c_{fs}", "v")
            else:
                for fs in range(16):
                    iq = "sy" if fs % 2 == 0 else "s"
                    if emit:
                        tgt, val = marks["wih_free"]
                        if fs < 2:
                            eng[iq].wait_ge(sem[tgt], val)
                            eng["g"].wait_ge(sem[tgt], val)
                        if fs >= 2:
                            tgt, val = marks[f"wi1c_{fs - 2}"]
                            eng[iq].wait_ge(sem[tgt], val)
                            tgt, val = marks[f"wo1c_{fs - 2}"]
                            eng["g"].wait_ge(sem[tgt], val)
                    dma(iq, f"dwq{fs % 2}",
                        V("wist" if fs % 2 == 0 else "wist1"), wi_sb[16 + fs])
                    mark(f"wi1d_{fs}", f"dwq{fs % 2}")
                    wt("v", f"wi1d_{fs}")
                    op("v", "tensor_copy",
                       V("wih")[:, fs * 1024:(fs + 1) * 1024],
                       V("wist" if fs % 2 == 0 else "wist1"))
                    mark(f"wi1c_{fs}", "v")
                    dma("g", f"dtb{fs % 2}", V("wost" if fs % 2 == 0 else "wost1"),
                        wo_sb[(16 + fs) * 128:(17 + fs) * 128])
                    mark(f"wo1d_{fs}", f"dtb{fs % 2}")
                    wt("v", f"wo1d_{fs}")
                    op("v", "tensor_copy",
                       V("woh")[:, fs * 1024:(fs + 1) * 1024],
                       V("wost" if fs % 2 == 0 else "wost1"))
                    mark(f"wo1c_{fs}", "v")
            for qt in range(16):
                t0 = qt * 128
                qb = qt % 2
                for fli in range(17):
                    if fli < 16:
                        fl = fli
                        fs = sp * 16 + fl
                        pb = fl % 2
                        if emit:
                            if sp == 0:
                                tgt, val = marks[f"wo0c_{fl}"]
                                eng["t"].wait_ge(sem[tgt], val)
                            if sp == 1:
                                tgt, val = marks[f"wi1c_{fl}"]
                                eng["t"].wait_ge(sem[tgt], val)
                                tgt, val = marks[f"wo1c_{fl}"]
                                eng["t"].wait_ge(sem[tgt], val)
                            if fl >= 2:
                                tgt, val = marks[f"relu_{sp}_{qt}_{fl - 2}"]
                                eng["t"].wait_ge(sem[tgt], val)
                            if fl == 0 and (sp * 16 + qt) >= 2:
                                k = sp * 16 + qt - 2
                                tgt, val = marks[f"fevac_{k // 16}_{k % 16}"]
                                eng["t"].wait_ge(sem[tgt], val)
                        for dc in range(DC):
                            op("t", "matmul", PV(f"pwi{pb}"),
                               V("wih")[:, fl * 1024 + dc * 128: fl * 1024
                                        + (dc + 1) * 128],
                               V("xnT_own")[:, dc * SH + t0: dc * SH + t0
                                            + 128],
                               start=(dc == 0), stop=False, _inc=0)
                        op("t", "matmul", PV(f"pwi{pb}"),
                           V("lbbf")[0:ER, fs * 128:(fs + 1) * 128],
                           V("tp")[0:ER, t0:t0 + 128], start=False, stop=True,
                           _inc=DC + 1)
                        mark(f"wips_{sp}_{qt}_{fl}", "t")
                        wt("s", f"wips_{sp}_{qt}_{fl}")
                        op("s", "activation", V(f"inter{pb}"), PV(f"pwi{pb}"),
                           AF.Relu, bias=V("wib_t")[:, fs:fs + 1])
                        mark(f"relu_{sp}_{qt}_{fl}", "s")
                    if fli >= 1:
                        fl = fli - 1
                        pb = fl % 2
                        wt("t", f"relu_{sp}_{qt}_{fl}")
                        for j in range(2):
                            op("t", "matmul",
                               PV(f"ptok{qb}")[:, j * 512:(j + 1) * 512],
                               V(f"inter{pb}"),
                               V("woh")[:, fl * 1024 + j * 512: fl * 1024
                                        + (j + 1) * 512],
                               start=(fl == 0), stop=(fl == 15))
                        mark(f"wops_{sp}_{qt}_{fl}", "t")
                mark(f"ffps_{sp}_{qt}", "t")
                wt("v", f"ffps_{sp}_{qt}")
                if emit and (sp * 16 + qt) >= 2:
                    k = sp * 16 + qt - 2
                    tgt, val = marks[f"fsp_{k // 16}_{k % 16}"]
                    eng["v"].wait_ge(sem[tgt], val)
                if emit and sp == 0 and qt == 0:
                    for t in ("dsv0", "dsv1"):
                        tgt, val = marks[f"spill_total_{t}"]
                        if val > 0:
                            eng["v"].wait_ge(sem[tgt], val)
                if sp == 0:
                    op("v", "tensor_copy", V(f"ffv{qb}"), PV(f"ptok{qb}"))
                else:
                    op("v", "tensor_add", V(f"ffv{qb}"), PV(f"ptok{qb}"),
                       V("wob_bc"))
                mark(f"fevac_{sp}_{qt}", "v")
                wt("g", f"fevac_{sp}_{qt}")
                tgt_sp = ffn0_sp if sp == 0 else ffn1_sp
                dma("g", f"dff{qb}", tgt_sp[t0:t0 + 128, :], V(f"ffv{qb}"))
                mark(f"fsp_{sp}_{qt}", f"dff{qb}")
            if sp == 0:
                mark("wih_free", "t")
        if not emit:
            for t in ("dff0", "dff1"):
                marks[f"bp_ff_{t}"] = (t, cnt[t])
        barrier("Bp")

        def stream_head(h):
            p, side = h // 2, h % 2
            bi = h % 2
            if emit and h >= 2:
                for mk in (f"hcons_t_{h - 2}", f"hcons_v_{h - 2}",
                           f"hcons_g_{h - 2}"):
                    tgt, val = marks[mk]
                    if val > 0:
                        eng["sy"].wait_ge(sem[tgt], val)
                        eng["g"].wait_ge(sem[tgt], val)
            for qr in range(4):
                dma("sy", f"dta{bi}", V(f"t1b{bi}")[qr * 32:(qr + 1) * 32, :],
                    t1_in[h, qr * 32:(qr + 1) * 32, :])
                dma("g", f"dtb{bi}", V(f"t2b{bi}")[qr * 32:(qr + 1) * 32, :],
                    t2_in[h, qr * 32:(qr + 1) * 32, :])
            dma("g", f"dst{bi}", V(f"ktb{bi}")[0:64, :],
                kT_sp[p, side * 64:side * 64 + 64, :])
            dma("g", f"dst{bi}", V(f"veb{bi}"),
                v_sp[p, :, side * 4096:(side + 1) * 4096])
            dma("g", f"dst{bi}", V(f"qtb{bi}")[0:64, :],
                qT_sp[p, side * 64:side * 64 + 64, :])
            mark(f"hs1_{h}", f"dta{bi}")
            mark(f"hs2_{h}", f"dtb{bi}")
            mark(f"hs3_{h}", f"dst{bi}")

        if not emit:
            for t in ("dsp0", "dsp1", "dsv0", "dsv1"):
                marks[f"spill_total_{t}"] = (t, cnt[t])
        else:
            for t in ("dsp0", "dsp1", "dsv0", "dsv1"):
                tgt, val = marks[f"spill_total_{t}"]
                if val > 0:
                    eng["g"].wait_ge(sem[tgt], val)
        stream_head(0)
        stream_head(1)
        for hc in range(DC):
            if emit and hc == 0:
                for t in ("dff0", "dff1"):
                    tgt, val = marks[f"bp_ff_{t}"]
                    if val > 0:
                        eng["s"].wait_ge(sem[tgt], val)
            if emit and hc >= 2:
                tgt, val = marks[f"woc_{hc - 2}"]
                eng["sy"].wait_ge(sem[tgt], val)
            dma("sy", f"dwq{hc % 2}", V(f"wouts{hc % 2}"), wout_sb[hc])
            mark(f"wod_{hc}", f"dwq{hc % 2}")
            wt("s", f"wod_{hc}")
            op("s", "copy", V("wobf2")[:, hc * D:(hc + 1) * D],
               V(f"wouts{hc % 2}"))
            mark(f"woc_{hc}", "s")
        mark("woutc", "s")

        def kept_tiles(h, qtp):
            theta = 80.0 * (2.0 ** (h / 2.0))
            qlo, qhi = qtp * 1024, qtp * 1024 + 1023
            ks = []
            for kc in range(32):
                rlo, rhi = 128 * kc, 128 * kc + 127
                if kc < 16:
                    dmin = max(0, rlo - qhi, qlo - rhi)
                else:
                    d0 = rlo - qhi
                    d1 = qlo + 4096 - rhi
                    dmin = min(d0, d1)
                if dmin <= theta:
                    ks.append(kc)
            return ks

        for h in range(H):
            bi = h % 2
            p, side = h // 2, h % 2
            if emit:
                for key, e in ((f"hs1_{h}", "v"), (f"hs2_{h}", "v"),
                               (f"hs1_{h}", "g"), (f"hs2_{h}", "g"),
                               (f"hs3_{h}", "t")):
                    tgt, val = marks[key]
                    eng[e].wait_ge(sem[tgt], val)
            for qtp in range(2):
                q0 = qtp * 1024
                ks = kept_tiles(h, qtp)
                nk = len(ks)
                pks = (kept_tiles(h, qtp - 1) if qtp else
                       (kept_tiles(h - 1, 1) if h else None))
                for kci in range(nk + 1):
                    if kci < nk:
                        kc = ks[kci]
                        sk = kci % 3
                        if emit:
                            if kci >= 3:
                                tgt, val = marks[
                                    f"exp_{h}_{qtp}_{ks[kci - 3]}"]
                                eng["t"].wait_ge(sem[tgt], val)
                            elif pks is not None:
                                ph, pq = (h, qtp - 1) if qtp else (h - 1, 1)
                                tgt, val = marks[
                                    f"exp_{ph}_{pq}_{pks[kci - 3]}"]
                                eng["t"].wait_ge(sem[tgt], val)
                                if kci == 0:
                                    tgt, val = marks[f"anorm_{ph}_{pq}"]
                                    eng["t"].wait_ge(sem[tgt], val)
                        for j in range(2):
                            op("t", "matmul",
                               PV(f"pss{sk}")[:, j * 512:(j + 1) * 512],
                               V(f"ktb{bi}")[0:64, kc * 128:(kc + 1) * 128],
                               V(f"qtb{bi}")[0:64, q0 + j * 512: q0 + (j + 1)
                                             * 512],
                               start=True, stop=True)
                        mark(f"sps_{h}_{qtp}_{kc}", "t")
                        wt("v", f"sps_{h}_{qtp}_{kc}")
                        tb = V(f"t1b{bi}") if kc < 16 else V(f"t2b{bi}")
                        jj0 = TC0 + q0 - 128 * (kc % 16)
                        op("v", "tensor_tensor", PV(f"pss{sk}"),
                           PV(f"pss{sk}"), tb[:, jj0:jj0 + 1024], ALU.add)
                        mark(f"badd_{h}_{qtp}_{kc}", "v")
                        wt("s", f"badd_{h}_{qtp}_{kc}")
                        if emit and kci >= 3:
                            tgt, val = marks[f"pv_{h}_{qtp}_{ks[kci - 3]}"]
                            eng["s"].wait_ge(sem[tgt], val)
                        op("s", "activation", V(f"esb{kci % 3}"),
                           PV(f"pss{sk}"), AF.Exp)
                        mark(f"exp_{h}_{qtp}_{kc}", "s")
                    if kci >= 1:
                        kc = ks[kci - 1]
                        wt("t", f"exp_{h}_{qtp}_{kc}")
                        for j in range(2):
                            op("t", "matmul", PV(f"pa{j}"),
                               V(f"veb{bi}")[:, kc * 128:(kc + 1) * 128],
                               V(f"esb{(kci - 1) % 3}")[:, j * 512:(j + 1)
                                                        * 512],
                               start=(kci == 1), stop=(kci == nk))
                        mark(f"pv_{h}_{qtp}_{kc}", "t")
                mark(f"pvdone_{h}_{qtp}", "t")
                wt("v", f"pvdone_{h}_{qtp}")
                for j in range(2):
                    op("v", "reciprocal", V("rden")[0:1, j * 512:(j + 1)
                                                    * 512],
                       PV(f"pa{j}")[64:65, :])
                mark(f"rden_{h}_{qtp}", "v")
                wt("g", f"rden_{h}_{qtp}")
                if emit and (h + qtp) > 0:
                    ph, pq = (h, qtp - 1) if qtp else (h - 1, 1)
                    tgt, val = marks[f"bcr_{ph}_{pq}"]
                    eng["g"].wait_ge(sem[tgt], val)
                dma("g", "dbc", bc_scr, V("rden")[0:1, :])
                mark(f"bcw_{h}_{qtp}", "dbc")
                wt("g", f"bcw_{h}_{qtp}")
                dma("g", "dbc", V("bcb"),
                    bass.AP(bc_scr.tensor, 0, [[0, 64], [1, 1024]]))
                mark(f"bcr_{h}_{qtp}", "dbc")
                wt("v", f"bcr_{h}_{qtp}")
                for j in range(2):
                    op("v", "tensor_tensor",
                       V("aoT")[side * 64:side * 64 + 64,
                                p * SH + q0 + j * 512: p * SH + q0 + (j + 1)
                                * 512],
                       PV(f"pa{j}")[0:64, :],
                       V("bcb")[0:64, j * 512:(j + 1) * 512], ALU.mult)
                mark(f"anorm_{h}_{qtp}", "v")
            mark(f"hcons_t_{h}", "t")
            mark(f"hcons_v_{h}", "v")
            mark(f"hcons_g_{h}", "g")
            if h + 2 < H:
                stream_head(h + 2)
        if emit:
            for key, e in (("woutc", "t"), (f"anorm_{H - 1}_1", "t"),
                           (f"exp_{H - 1}_1_31", "t")):
                tgt, val = marks[key]
                eng[e].wait_ge(sem[tgt], val)
        for qt in range(16):
            qb = qt % 2
            t0 = qt * 128
            if emit and qt >= 2:
                tgt, val = marks[f"yevac_{qt - 2}"]
                eng["t"].wait_ge(sem[tgt], val)
            for hc in range(DC):
                for j in range(2):
                    op("t", "matmul",
                       PV(f"pty{qb}")[:, j * 512:(j + 1) * 512],
                       V("aoT")[:, hc * SH + t0: hc * SH + t0 + 128],
                       V("wobf2")[:, hc * D + j * 512: hc * D + (j + 1)
                                  * 512],
                       start=(hc == 0), stop=(hc == DC - 1))
            mark(f"yps_{qt}", "t")
            wt("v", f"yps_{qt}")
            if emit and qt >= 2:
                tgt, val = marks[f"ysp_{qt - 2}"]
                eng["v"].wait_ge(sem[tgt], val)
            op("v", "tensor_add", V(f"ysv{qb}"), PV(f"pty{qb}"),
               V("bout_bc"))
            mark(f"yevac_{qt}", "v")
            wt("g", f"yevac_{qt}")
            dma("g", f"dff{qb}", aty_sp[t0:t0 + 128, :], V(f"ysv{qb}"))
            mark(f"ysp_{qt}", f"dff{qb}")
        barrier("C")

        if not emit:
            for t in ("dff0", "dff1"):
                marks[f"ff_total_{t}"] = (t, cnt[t])
        else:
            for t in ("dff0", "dff1"):
                tgt, val = marks[f"ff_total_{t}"]
                if val > 0:
                    eng["sy"].wait_ge(sem[tgt], val)
        for tc in range(16):
            bi = tc % 2
            if emit and tc >= 2:
                tgt, val = marks[f"osum_{tc - 2}"]
                eng["sy"].wait_ge(sem[tgt], val)
            r0 = tc * 128
            dma("sy", f"dx{bi}", V(f"ya{bi}"), aty_sp[r0:r0 + 128, :])
            dma("sy", f"dx{bi}", V(f"fa{bi}"), ffn0_sp[r0:r0 + 128, :])
            dma("sy", f"dx{bi}", V(f"fb{bi}"), ffn1_sp[r0:r0 + 128, :])
            dma("sy", f"dx{bi}", V(f"xra{bi}"), x_in[r0:r0 + 128, :])
            mark(f"ein_{tc}", f"dx{bi}")
            wt("v", f"ein_{tc}")
            if emit and tc >= 2:
                tgt, val = marks[f"out_{tc - 2}"]
                eng["v"].wait_ge(sem[tgt], val)
            op("v", "tensor_add", V(f"sm{bi}"), V(f"ya{bi}"), V(f"fa{bi}"))
            op("v", "tensor_add", V(f"sm{bi}"), V(f"sm{bi}"), V(f"fb{bi}"))
            op("v", "tensor_add", V(f"ob{bi}"), V(f"sm{bi}"), V(f"xra{bi}"))
            mark(f"osum_{tc}", "v")
            wt("g", f"osum_{tc}")
            dma("g", f"do{bi}", out[r0:r0 + 128, :], V(f"ob{bi}"))
            mark(f"out_{tc}", f"do{bi}")
        if emit:
            eng["g"].wait_ge(sem["do0"], cnt["do0"])
            eng["g"].wait_ge(sem["do1"], cnt["do1"])

    L = {}

    def place(name, off, part, ncols, dtype, prow=0):
        L[name] = (off, part, ncols, dtype, prow)

    place("idbf", 0, 128, 128, BF16)
    place("idf32", 256, 128, 128, F32)
    place("ecols", 768, 128, ER, F32)
    place("bq_t", 896, 128, DC, F32)
    place("bk_t", 928, 128, DC, F32)
    place("bqs_t", 960, 128, DC, F32)
    place("wib_t", 1024, 128, 32, F32)
    place("gb_t", 1152, E, 1, F32)
    place("ones1", 1216, 1, 128, BF16)
    place("bvf", 1536, 1, D, F32)
    place("bvbf", 5632, 1, D, BF16)
    place("gwf", 7680, 128, DC * E, F32)
    place("gwbf", 7936, 128, DC * E, BF16)
    place("laf", 8192, 128, DC * ER, F32)
    place("labf", 9216, 128, DC * ER, BF16)
    place("lbbf", 9728, ER, F, BF16)
    place("wob_bc", 26112, 128, D, F32)
    place("bout_bc", 30208, 128, D, F32)
    place("xnT_own", 34304, 128, DC * SH, BF16)
    place("xnT_oth", 67072, 128, DC * SH, BF16)
    place("xa0", 99840, 128, D, F32)
    place("xa1", 103936, 128, D, F32)
    place("xc", 108032, 128, D, F32)
    place("sq", 112128, 128, D, F32)
    place("xnc0", 116224, 128, D, BF16)
    place("xnc1", 118272, 128, D, BF16)
    place("st1", 120320, 128, 8, F32)
    place("lbf", 120576, ER, F, F32)
    place("wf320", 99840, 128, 3 * D, F32)
    place("wf321", 112128, 128, 3 * D, F32)
    place("wbf0", 124416, 128, 3 * D, BF16)
    place("wbf1", 130560, 128, 3 * D, BF16)
    place("ksp0", 136704, 128, 512, BF16)
    place("ksp1", 137728, 128, 512, BF16)
    place("qsp0", 138752, 128, 512, BF16)
    place("qsp1", 139776, 128, 512, BF16)
    place("vsp0", 140800, 128, 4096, BF16)
    place("vsp1", 185984, 128, 4096, BF16)
    place("wih", 149120, 128, 16 * 1024, BF16)
    place("wistp", 181888, 128, D, F32)
    place("wostp", 185984, 128, D, F32)
    place("woh", 99840, 128, 16 * 1024, BF16)
    place("wist", 132608, 128, D, F32)
    place("wist1", 136704, 128, D, F32)
    place("wost", 140800, 128, D, F32)
    place("wost1", 144896, 128, D, F32)
    place("inter0", 181888, 128, 128, BF16)
    place("inter1", 182144, 128, 128, BF16)
    place("ffv0", 182400, 128, D, F32)
    place("ffv1", 186496, 128, D, F32)
    place("glog", 67072, E, SH, F32)
    place("gmx", 75264, 128, E, F32)
    place("gmi", 75296, 128, E, U32)
    place("gif", 75328, 128, 1, F32)
    place("mtok", 75392, 128, ER, BF16)
    place("maskT", 75520, ER, SH, BF16)
    place("tp", 79616, ER, SH, BF16)
    place("t1b0", 34304, 128, TW, F32)
    place("t1b1", 50176, 128, TW, F32)
    place("t2b0", 66048, 128, TW, F32)
    place("t2b1", 81920, 128, TW, F32)
    place("ktb0", 97792, 128, S, BF16)
    place("ktb1", 105984, 128, S, BF16)
    place("veb0", 114176, 128, 4096, BF16)
    place("veb1", 122368, 128, 4096, BF16)
    place("qtb0", 130560, 128, SH, BF16)
    place("qtb1", 134656, 128, SH, BF16)
    place("esb0", 138752, 128, 1024, BF16)
    place("esb1", 140800, 128, 1024, BF16)
    place("esb2", 192128, 128, 1024, BF16)
    place("rden", 97792, 1, 1024, F32, prow=64)
    place("bcb", 105984, 64, 1024, F32, prow=64)
    place("aoT", 142976, 128, DC * SH, BF16)
    place("wobf2", 175744, 128, DC * D, BF16)
    place("wouts0", 17920, 128, D, F32)
    place("wouts1", 22016, 128, D, F32)
    place("ysv0", 130560, 128, D, F32)
    place("ysv1", 134656, 128, D, F32)
    place("ya0", 34304, 128, D, F32)
    place("ya1", 38400, 128, D, F32)
    place("fa0", 42496, 128, D, F32)
    place("fa1", 46592, 128, D, F32)
    place("fb0", 50688, 128, D, F32)
    place("fb1", 54784, 128, D, F32)
    place("xra0", 58880, 128, D, F32)
    place("xra1", 62976, 128, D, F32)
    place("sm0", 67072, 128, D, F32)
    place("sm1", 71168, 128, D, F32)
    place("ob0", 75264, 128, D, F32)
    place("ob1", 79360, 128, D, F32)

    PL = {
        "tr0": (0, 128, 1024, BF16), "tr1": (512, 128, 1024, BF16),
        "pk0": (1024, 128, 512, F32), "pk1": (1536, 128, 512, F32),
        "pq0": (2048, 128, 512, F32), "pq1": (2560, 128, 512, F32),
        "pv0": (3072, 128, 128, F32), "pv1": (3584, 128, 128, F32),
        "pg0": (2048, E, 512, F32), "pg1": (2560, E, 512, F32),
        "pgt0": (3072, 128, E, F32), "pgt1": (3080, 128, E, F32),
        "pmt0": (3088, ER, 128, BF16), "pmt1": (3152, ER, 128, BF16),
        "pt320": (2048, ER, 512, F32), "pt321": (2560, ER, 512, F32),
        "pwi0": (3456, 128, 128, F32), "pwi1": (3840, 128, 128, F32),
        "ptok0": (0, 128, 1024, F32), "ptok1": (1024, 128, 1024, F32),
        "pss0": (0, 128, 1024, F32), "pss1": (1024, 128, 1024, F32),
        "pss2": (2048, 128, 1024, F32),
        "pa0": (3072, 128, 512, F32), "pa1": (3584, 128, 512, F32),
        "pty0": (0, 128, 1024, F32), "pty1": (1024, 128, 1024, F32),
    }

    es = ExitStack()
    with es:
        sbA = es.enter_context(nc.sbuf_tensor("sbA", [128, ARENA // 4], F32))
        psA = es.enter_context(nc.psum_tensor("psA", [128, 4096], F32))
        sems = {}
        for sname in CH + DM:
            sems[sname] = es.enter_context(nc.semaphore(f"sem_{sname}"))
        block = es.enter_context(nc.Block())

        def V(name):
            off, part, ncols, dtype, prow = L[name]
            esz = 2 if dtype == BF16 else 4
            assert off % 4 == 0 and off + ncols * esz <= ARENA, name
            v = sbA[prow:prow + part, off // 4: (off + ncols * esz + 3) // 4]
            if dtype != F32:
                v = v.bitcast(dtype)
            if v.shape[1] != ncols:
                v = v[:, 0:ncols]
            return v

        def PV(name):
            off, part, ncols, dtype = PL[name]
            esz = 2 if dtype == BF16 else 4
            v = psA[0:part, off: off + (ncols * esz + 3) // 4]
            if dtype != F32:
                v = v.bitcast(dtype)
            if v.shape[1] != ncols:
                v = v[:, 0:ncols]
            return v

        cx = {"emit": False, "cnt": {k: 0 for k in CH + DM},
              "eng": {k: None for k in CH + ["sy"]}, "sem": sems,
              "V": V, "PV": PV}
        prog(cx)

        class _DummyInstr:
            def then_inc(self, *a, **kw):
                return self

        class _DummyEng:
            def wait_ge(self, *a, **kw):
                return None

            def __getattr__(self, name):
                return lambda *a, **kw: _DummyInstr()

        def emit_for(engname, real):
            eng = {k: _DummyEng() for k in CH + ["sy"]}
            eng[engname] = real
            cx2 = {"emit": True, "cnt": {k: 0 for k in CH + DM},
                   "eng": eng, "sem": sems, "V": V, "PV": PV}
            prog(cx2)

        @block.tensor
        def _(t):
            emit_for("t", t)

        @block.vector
        def _(v):
            emit_for("v", v)

        @block.scalar
        def _(s):
            emit_for("s", s)

        @block.gpsimd
        def _(g):
            emit_for("g", g)

        @block.sync
        def _(sy):
            emit_for("sy", sy)

    return nc


_cache = {}


def kernel(**inputs):
    f32 = np.float32
    xf = np.ascontiguousarray(inputs["L0_attn_x"], dtype=f32)
    wqkv = np.ascontiguousarray(inputs["L0_attn_W_qkv"], dtype=f32)
    bqkv = np.ascontiguousarray(inputs["L0_attn_b_qkv"], dtype=f32)
    wout = np.ascontiguousarray(inputs["L0_attn_W_out"], dtype=f32)
    bout = np.ascontiguousarray(inputs["L0_attn_b_out"], dtype=f32)
    gw = np.ascontiguousarray(inputs["L0_ffn_gate_W"], dtype=f32)
    gb = np.ascontiguousarray(inputs["L0_ffn_gate_b"], dtype=f32)
    wi = np.ascontiguousarray(inputs["L0_ffn_wi_W"], dtype=f32)
    wib = np.ascontiguousarray(inputs["L0_ffn_wi_b"], dtype=f32)
    wo = np.ascontiguousarray(inputs["L0_ffn_wo_W"], dtype=f32)
    wob = np.ascontiguousarray(inputs["L0_ffn_wo_b"], dtype=f32)
    la = np.ascontiguousarray(inputs["L0_ffn_lora_A"], dtype=f32)
    lb = np.ascontiguousarray(inputs["L0_ffn_lora_B"], dtype=f32)

    slopes = np.power(np.float32(2.0),
                      (8.0 * np.arange(0, -H, -1, dtype=f32) / H).astype(f32))
    kk = np.arange(128)
    jj = np.arange(TW)
    base = (jj[None, :] - TC0) - kk[:, None]
    T1 = np.empty((H, 128, TW), dtype=f32)
    T2h = {0: np.empty((H, 128, TW), dtype=f32),
           1: np.empty((H, 128, TW), dtype=f32)}
    SHIFT = np.float32(64.0)
    for h in range(H):
        s = slopes[h]
        T1[h] = (-s) * np.abs(base).astype(f32) - SHIFT
        T2h[0][h] = (-s) * np.abs(base - 2048).astype(f32) - SHIFT
        T2h[1][h] = (-s) * np.abs(base + 2048).astype(f32) - SHIFT

    def pairize(w):
        return np.ascontiguousarray(
            w.reshape(8, 128, 8, 128).transpose(0, 3, 2, 1).reshape(
                8, 128, 1024))

    wq_p = pairize(wqkv[0:D] )
    wk_p = pairize(wqkv[D:2 * D])
    wv_p = pairize(wqkv[2 * D:3 * D])
    wqkv_sb = np.ascontiguousarray(
        np.concatenate([wq_p, wk_p, wv_p], axis=2))
    shared = {
        "wqkv_sb": wqkv_sb,
        "bq_sb": np.ascontiguousarray(bqkv[0:D].reshape(8, 128).T),
        "bk_sb": np.ascontiguousarray(bqkv[D:2 * D].reshape(8, 128).T),
        "bv_row": np.ascontiguousarray(bqkv[2 * D:3 * D][None, :]),
        "wout_sb": np.ascontiguousarray(wout.T.reshape(8, 128, D)),
        "boutbc": np.ascontiguousarray(
            np.broadcast_to(bout[None, :], (128, D))),
        "wi_sb": np.ascontiguousarray(
            wi.reshape(32, 128, 8, 128).transpose(0, 3, 2, 1).reshape(
                32, 128, D)),
        "wib_sb": np.ascontiguousarray(wib.reshape(32, 128).T),
        "wo_sb": np.ascontiguousarray(wo.T),
        "wobbc": np.ascontiguousarray(
            np.broadcast_to(wob[None, :], (128, D))),
        "gw_sb": np.ascontiguousarray(
            gw.T.reshape(8, 128, E).transpose(1, 0, 2).reshape(128, 8 * E)),
        "gb_sb": np.ascontiguousarray(gb[:, None]),
        "la_sb": np.ascontiguousarray(
            la.reshape(ER, 8, 128).transpose(2, 1, 0).reshape(128, 8 * ER)),
        "lb_sb": np.ascontiguousarray(lb.transpose(0, 2, 1).reshape(ER, F)),
        "T1": T1,
        "idbf": np.eye(128, dtype=ml_dtypes.bfloat16),
        "idf32": np.eye(128, dtype=f32),
        "ecols": np.ascontiguousarray(np.broadcast_to(
            np.repeat(np.arange(E, dtype=f32), R)[None, :], (128, ER))),
    }

    if "nc" not in _cache:
        _cache["nc"] = build_program()
    nc = _cache["nc"]

    in_maps = []
    for core in range(8):
        b, half = core // 2, core % 2
        m = dict(shared)
        m["x"] = np.ascontiguousarray(np.roll(xf[b], -half * SH, axis=0))
        m["T2"] = T2h[half]
        in_maps.append(m)

    res = run_bass_kernel_spmd(nc, in_maps, core_ids=list(range(8)))
    outf = np.empty((B, S, D), dtype=np.float32)
    for core in range(8):
        b, half = core // 2, core % 2
        outf[b, half * SH:(half + 1) * SH] = res.results[core]["out"]
    return outf


# version 11 (final)
# speedup vs baseline: 1.0086x; 1.0086x over previous
import numpy as np
import ml_dtypes
import concourse.bass as bass
import concourse.mybir as mybir
from concourse.bass_utils import run_bass_kernel_spmd
from contextlib import ExitStack

F32 = mybir.dt.float32
BF16 = mybir.dt.bfloat16
U32 = mybir.dt.uint32
AF = mybir.ActivationFunctionType
ALU = mybir.AluOpType
AX = mybir.AxisListType

B, S, D = 4, 4096, 1024
H = 16
SH = S // 2
F = 4096
E, R = 8, 4
ER = E * R
NP = 8
DC = 8
TW = 3968
TC0 = 1920
ARENA = 194560
CH = ["t", "v", "s", "g"]
DM = ["dw", "dwi", "dwq0", "dwq1", "dta0", "dta1", "dtb0", "dtb1",
      "dst0", "dst1", "dbc", "dsp0", "dsp1", "dsv0", "dsv1",
      "dff0", "dff1", "dx0", "dx1", "do0", "do1"]


def build_program():
    nc = bass.Bass("TRN2", target_bir_lowering=False)

    def din(name, shape, dtype=F32):
        return nc.dram_tensor(name, shape, dtype, kind="ExternalInput").ap()

    x_in = din("x", [S, D])
    wqkv_sb = din("wqkv_sb", [NP, 128, 3072])
    bq_sb = din("bq_sb", [128, DC])
    bk_sb = din("bk_sb", [128, DC])
    bv_row = din("bv_row", [1, D])
    wout_sb = din("wout_sb", [DC, 128, D])
    boutbc_in = din("boutbc", [128, D])
    wi_sb = din("wi_sb", [32, 128, D])
    wib_sb = din("wib_sb", [128, 32])
    wo_sb = din("wo_sb", [F, D])
    wobbc_in = din("wobbc", [128, D])
    gw_sb = din("gw_sb", [128, DC * E])
    gb_sb = din("gb_sb", [E, 1])
    la_sb = din("la_sb", [128, DC * ER])
    lb_sb = din("lb_sb", [ER, F])
    t1_in = din("T1", [H, 128, TW])
    t2_in = din("T2", [H, 128, TW])
    idbf_in = din("idbf", [128, 128], BF16)
    idf_in = din("idf32", [128, 128], F32)
    ec_in = din("ecols", [128, ER], F32)

    out = nc.dram_tensor("out", [SH, D], F32, kind="ExternalOutput").ap()

    kT_sp = nc.dram_tensor("kT_sp", [NP, 128, S], BF16).ap()
    v_sp = nc.dram_tensor("v_sp", [NP, 128, 2 * 4096], BF16).ap()
    qT_sp = nc.dram_tensor("qT_sp", [NP, 128, SH], BF16).ap()
    ffn0_sp = nc.dram_tensor("ffn0_sp", [SH, D], F32).ap()
    ffn1_sp = nc.dram_tensor("ffn1_sp", [SH, D], F32).ap()
    aty_sp = nc.dram_tensor("aty_sp", [SH, D], F32).ap()
    bc_scr = nc.dram_tensor("bc_scr", [1, 1024], F32).ap()

    marks = {}

    def prog(cx):
        emit, cnt, eng, sem, V, PV = (cx["emit"], cx["cnt"], cx["eng"],
                                      cx["sem"], cx["V"], cx["PV"])

        def op(e, f, *a, **kw):
            inc = kw.pop("_inc", 1)
            if emit:
                if cnt[e] > 0 and e != "t":
                    eng[e].wait_ge(sem[e], cnt[e])
                r = getattr(eng[e], f)(*a, **kw)
                if inc:
                    r.then_inc(sem[e], inc)
            cnt[e] += 1

        def dma(e, ds, outap, inap):
            if emit:
                eng[e].dma_start(out=outap, in_=inap).then_inc(sem[ds], 16)
            cnt[ds] += 16

        def mark(key, tgt):
            if not emit:
                marks[key] = (tgt, cnt[tgt])

        def wt(e, key):
            if emit:
                tgt, val = marks[key]
                if val > 0:
                    eng[e].wait_ge(sem[tgt], val)

        def barrier(name):
            if not emit:
                for t in CH:
                    marks[f"bar_{name}_{t}"] = (t, cnt[t])
            else:
                for e in CH + ["sy"]:
                    for t in CH:
                        if t == e:
                            continue
                        tgt, val = marks[f"bar_{name}_{t}"]
                        if val > 0:
                            eng[e].wait_ge(sem[tgt], val)

        dma("sy", "dw", V("idbf"), idbf_in)
        dma("sy", "dw", V("idf32"), idf_in)
        dma("sy", "dw", V("ecols"), ec_in)
        dma("sy", "dw", V("bq_t"), bq_sb)
        dma("sy", "dw", V("bk_t"), bk_sb)
        dma("sy", "dw", V("wib_t"), wib_sb)
        dma("sy", "dw", V("gb_t"), gb_sb)
        dma("sy", "dw", V("gwf"), gw_sb)
        dma("sy", "dw", V("laf"), la_sb)
        dma("sy", "dw", V("bvf"), bv_row)
        dma("sy", "dw", V("wob_bc"), wobbc_in)
        dma("sy", "dw", V("bout_bc"), boutbc_in)
        dma("sy", "dw", V("lbf"), lb_sb)
        mark("pre_dw", "dw")
        wt("v", "pre_dw")
        op("v", "memset", V("ones1"), 1.0)
        for bi in range(2):
            op("v", "memset", V(f"vsp{bi}")[:, 64:4096:128], 1.0)
            op("v", "memset",
               V(f"vsp{bi}").rearrange("p (tc c) -> p tc c", tc=32)[:, :,
                                                                   65:128],
               0.0)
        op("v", "tensor_scalar_mul", V("bqs_t"), V("bq_t"), 0.125)
        op("v", "tensor_copy", V("bvbf"), V("bvf"))
        op("v", "tensor_copy", V("gwbf"), V("gwf"))
        op("v", "tensor_copy", V("labf"), V("laf"))
        op("v", "tensor_copy", V("lbbf"), V("lbf"))
        mark("pre_conv", "v")
        for fs in range(16):
            if emit and fs >= 1:
                tgt, val = marks[f"wi0c_{fs - 1}"]
                eng["sy"].wait_ge(sem[tgt], val)
            dma("sy", "dwi", V("wistp"), wi_sb[fs])
            mark(f"wi0d_{fs}", "dwi")
            wt("g", f"wi0d_{fs}")
            op("g", "tensor_copy", V("wih")[:, fs * 1024:(fs + 1) * 1024],
               V("wistp"))
            mark(f"wi0c_{fs}", "g")
        mark("ffnw0", "g")

        for tc in range(32):
            bi = tc % 2
            if emit and tc >= 2:
                tgt, val = marks[f"xa_free_{tc - 2}"]
                eng["sy"].wait_ge(sem[tgt], val)
            dma("sy", f"dx{bi}", V(f"xa{bi}"), x_in[tc * 128:(tc + 1) * 128, :])
            mark(f"xa_in_{tc}", f"dx{bi}")
            wt("v", f"xa_in_{tc}")
            xa = V(f"xa{bi}")
            op("v", "reduce_sum", V("st1")[:, 0:1], xa, axis=AX.X)
            op("v", "tensor_scalar_mul", V("st1")[:, 1:2], V("st1")[:, 0:1],
               1.0 / D)
            op("v", "tensor_scalar_sub", V("xc"), xa, V("st1")[:, 1:2])
            mark(f"xa_free_{tc}", "v")
            if emit:
                eng["s"].wait_ge(sem["v"], cnt["v"])
            op("s", "activation", V("sq"), V("xc"), AF.Square,
               accum_out=V("st1")[:, 2:3])
            if emit:
                eng["v"].wait_ge(sem["s"], cnt["s"])
            op("v", "tensor_scalar", V("st1")[:, 3:4], V("st1")[:, 2:3],
               1.0 / D, 1e-5, ALU.mult, ALU.add)
            if emit:
                eng["s"].wait_ge(sem["v"], cnt["v"])
            op("s", "activation", V("st1")[:, 4:5], V("st1")[:, 3:4], AF.Sqrt)
            if emit:
                eng["v"].wait_ge(sem["s"], cnt["s"])
            op("v", "reciprocal", V("st1")[:, 5:6], V("st1")[:, 4:5])
            op("v", "tensor_scalar_mul", V(f"xnc{bi}"), V("xc"),
               V("st1")[:, 5:6])
            mark(f"xnc_{tc}", "v")
            wt("t", f"xnc_{tc}")
            if emit and tc >= 2:
                tgt, val = marks[f"trcopy_{tc - 2}"]
                eng["t"].wait_ge(sem[tgt], val)
            for dc in range(DC):
                op("t", "transpose", PV(f"tr{bi}")[:, dc * 128:(dc + 1) * 128],
                   V(f"xnc{bi}")[:, dc * 128:(dc + 1) * 128], V("idbf"))
            mark(f"tr_{tc}", "t")
            wt("v", f"tr_{tc}")
            own = tc < 16
            xnT = V("xnT_own") if own else V("xnT_oth")
            col = (tc if own else tc - 16) * 128
            dst = xnT.rearrange("p (dc t) -> p dc t", dc=DC)[:, :,
                                                             col:col + 128]
            src = PV(f"tr{bi}").rearrange("p (dc t) -> p dc t", dc=DC)
            op("v", "tensor_copy", dst, src)
            mark(f"trcopy_{tc}", "v")
        barrier("A")

        def xnT_sl(dc, t0, tlen):
            if t0 < SH:
                return V("xnT_own")[:, dc * SH + t0: dc * SH + t0 + tlen]
            t0 -= SH
            return V("xnT_oth")[:, dc * SH + t0: dc * SH + t0 + tlen]

        for p in range(NP):
            bi = p % 2
            if emit and p >= 2:
                tgt, val = marks[f"wconv_{p - 2}"]
                eng["sy"].wait_ge(sem[tgt], val)
            dma("sy", f"dwq{bi}", V(f"wf32{bi}"), wqkv_sb[p])
            mark(f"wdma_{p}", f"dwq{bi}")
            wt("s", f"wdma_{p}")
            if emit and p >= 2:
                tgt, val = marks[f"wbf_free_{p - 2}"]
                eng["s"].wait_ge(sem[tgt], val)
            op("s", "mul", V(f"wbf{bi}")[:, 0:D], V(f"wf32{bi}")[:, 0:D],
               0.125)
            op("s", "copy", V(f"wbf{bi}")[:, D:3 * D],
               V(f"wf32{bi}")[:, D:3 * D])
            mark(f"wconv_{p}", "s")
            wt("t", f"wconv_{p}")
            wt("t", "pre_conv")
            for st in range(8):
                pb = st % 2
                if emit and st >= 2:
                    tgt, val = marks[f"kevac_{p}_{st - 2}"]
                    eng["t"].wait_ge(sem[tgt], val)
                for dc in range(DC):
                    op("t", "matmul", PV(f"pk{pb}"),
                       V(f"wbf{bi}")[:, D + dc * 128:D + (dc + 1) * 128],
                       xnT_sl(dc, st * 512, 512),
                       start=(dc == 0), stop=(dc == DC - 1),
                       _inc=(DC if dc == DC - 1 else 0))
                mark(f"kps_{p}_{st}", "t")
                wt("s", f"kps_{p}_{st}")
                if emit and st >= 2:
                    tgt, val = marks[f"kspill_{p}_{st - 2}"]
                    eng["s"].wait_ge(sem[tgt], val)
                op("s", "add", V(f"ksp{pb}"), PV(f"pk{pb}"),
                   V("bk_t")[:, p:p + 1])
                mark(f"kevac_{p}_{st}", "s")
                wt("g", f"kevac_{p}_{st}")
                dma("g", f"dsp{pb}", kT_sp[p, :, st * 512:(st + 1) * 512],
                    V(f"ksp{pb}"))
                mark(f"kspill_{p}_{st}", f"dsp{pb}")
            for st in range(4):
                pb = st % 2
                if emit and st >= 2:
                    tgt, val = marks[f"qevac_{p}_{st - 2}"]
                    eng["t"].wait_ge(sem[tgt], val)
                for dc in range(DC):
                    op("t", "matmul", PV(f"pq{pb}"),
                       V(f"wbf{bi}")[:, dc * 128:(dc + 1) * 128],
                       xnT_sl(dc, st * 512, 512),
                       start=(dc == 0), stop=(dc == DC - 1),
                       _inc=(DC if dc == DC - 1 else 0))
                mark(f"qps_{p}_{st}", "t")
                wt("s", f"qps_{p}_{st}")
                if emit and st >= 2:
                    tgt, val = marks[f"qspill_{p}_{st - 2}"]
                    eng["s"].wait_ge(sem[tgt], val)
                op("s", "add", V(f"qsp{pb}"), PV(f"pq{pb}"),
                   V("bqs_t")[:, p:p + 1])
                mark(f"qevac_{p}_{st}", "s")
                wt("g", f"qevac_{p}_{st}")
                dma("g", f"dsp{pb}", qT_sp[p, :, st * 512:(st + 1) * 512],
                    V(f"qsp{pb}"))
                mark(f"qspill_{p}_{st}", f"dsp{pb}")
            if emit and p >= 1:
                for side in range(2):
                    tgt, val = marks[f"vspill_{p - 1}_{side}"]
                    eng["v"].wait_ge(sem[tgt], val)
            for tc in range(32):
                pb = tc % 2
                if emit and tc >= 2:
                    tgt, val = marks[f"vevac_{p}_{tc - 2}"]
                    eng["t"].wait_ge(sem[tgt], val)
                for dc in range(DC):
                    op("t", "matmul", PV(f"pv{pb}"),
                       xnT_sl(dc, tc * 128, 128),
                       V(f"wbf{bi}")[:, 2 * D + dc * 128:2 * D + (dc + 1)
                                     * 128],
                       start=(dc == 0), stop=False, _inc=0)
                op("t", "matmul", PV(f"pv{pb}"), V("ones1"),
                   V("bvbf")[0:1, p * 128:(p + 1) * 128],
                   start=False, stop=True, _inc=DC + 1)
                mark(f"vps_{p}_{tc}", "t")
                wt("v", f"vps_{p}_{tc}")
                op("v", "tensor_copy", V("vsp0")[:, tc * 128:tc * 128 + 64],
                   PV(f"pv{pb}")[:, 0:64])
                op("v", "tensor_copy", V("vsp1")[:, tc * 128:tc * 128 + 64],
                   PV(f"pv{pb}")[:, 64:128])
                mark(f"vevac_{p}_{tc}", "v")
            mark(f"vdone_{p}", "v")
            for side in range(2):
                wt("g", f"vdone_{p}")
                dma("g", f"dsv{side}", v_sp[p, :, side * 4096:(side + 1) * 4096],
                    V(f"vsp{side}"))
                mark(f"vspill_{p}_{side}", f"dsv{side}")
            mark(f"wbf_free_{p}", "t")
        barrier("B")

        wt("t", "pre_conv")
        for qt in range(4):
            pb = qt % 2
            if emit and qt >= 2:
                tgt, val = marks[f"gevac_{qt - 2}"]
                eng["t"].wait_ge(sem[tgt], val)
            for dc in range(DC):
                op("t", "matmul", PV(f"pg{pb}")[0:E, :],
                   V("gwbf")[:, dc * E:(dc + 1) * E],
                   V("xnT_own")[:, dc * SH + qt * 512: dc * SH + qt * 512
                                + 512],
                   start=(dc == 0), stop=(dc == DC - 1))
            mark(f"gps_{qt}", "t")
            wt("s", f"gps_{qt}")
            op("s", "add", V("glog")[0:E, qt * 512:(qt + 1) * 512],
               PV(f"pg{pb}")[0:E, :], V("gb_t")[0:E, 0:1])
            mark(f"gevac_{qt}", "s")
        for tc in range(16):
            pb = tc % 2
            if emit:
                tgt, val = marks[f"gevac_{tc // 4}"]
                eng["t"].wait_ge(sem[tgt], val)
                if tc >= 2:
                    tgt, val = marks[f"amax_{tc - 2}"]
                    eng["t"].wait_ge(sem[tgt], val)
            op("t", "transpose", PV(f"pgt{pb}"),
               V("glog")[0:E, tc * 128:(tc + 1) * 128], V("idf32")[0:E, 0:E])
            mark(f"gtr_{tc}", "t")
            wt("v", f"gtr_{tc}")
            op("v", "max", V("gmx"), PV(f"pgt{pb}"))
            op("v", "max_index", V("gmi"), V("gmx"), PV(f"pgt{pb}"))
            mark(f"amax_{tc}", "v")
            op("v", "tensor_copy", V("gif"), V("gmi")[:, 0:1])
            op("v", "tensor_single_scalar", V("mtok"), V("ecols"),
               V("gif")[:, 0:1], ALU.is_equal)
            mark(f"mtok_{tc}", "v")
            wt("t", f"mtok_{tc}")
            if emit and tc >= 2:
                tgt, val = marks[f"mcp_{tc - 2}"]
                eng["t"].wait_ge(sem[tgt], val)
            op("t", "transpose", PV(f"pmt{pb}")[0:ER, :], V("mtok"),
               V("idbf"))
            mark(f"mT_{tc}", "t")
            wt("v", f"mT_{tc}")
            op("v", "tensor_copy", V("maskT")[0:ER, tc * 128:(tc + 1) * 128],
               PV(f"pmt{pb}")[0:ER, :])
            mark(f"mcp_{tc}", "v")
        for qt in range(4):
            pb = qt % 2
            if emit:
                tgt, val = marks["gevac_3"]
                eng["t"].wait_ge(sem[tgt], val)
                if qt >= 2:
                    tgt, val = marks[f"tp_{qt - 2}"]
                    eng["t"].wait_ge(sem[tgt], val)
            for dc in range(DC):
                op("t", "matmul", PV(f"pt32{pb}")[0:ER, :],
                   V("labf")[:, dc * ER:(dc + 1) * ER],
                   V("xnT_own")[:, dc * SH + qt * 512: dc * SH + qt * 512
                                + 512],
                   start=(dc == 0), stop=(dc == DC - 1))
            mark(f"tps_{qt}", "t")
            wt("v", f"tps_{qt}")
            if emit:
                tgt, val = marks["mcp_15"]
                eng["v"].wait_ge(sem[tgt], val)
            op("v", "tensor_mul", V("tp")[0:ER, qt * 512:(qt + 1) * 512],
               PV(f"pt32{pb}")[0:ER, :],
               V("maskT")[0:ER, qt * 512:(qt + 1) * 512])
            mark(f"tp_{qt}", "v")

        for sp in range(2):
            if sp == 0:
                wt("t", "ffnw0")
                if emit:
                    for t in ("dsv0", "dsv1", "dsp0", "dsp1"):
                        tgt, val = marks[f"spill_total_{t}"]
                        if val > 0:
                            eng["g"].wait_ge(sem[tgt], val)
                for fs in range(16):
                    if emit and fs >= 2:
                        tgt, val = marks[f"wo0c_{fs - 2}"]
                        eng["g"].wait_ge(sem[tgt], val)
                    dma("g", f"dtb{fs % 2}", V("wost" if fs % 2 == 0 else "wost1"),
                        wo_sb[fs * 128:(fs + 1) * 128])
                    mark(f"wo0d_{fs}", f"dtb{fs % 2}")
                    wt("v", f"wo0d_{fs}")
                    op("v", "tensor_copy",
                       V("woh")[:, fs * 1024:(fs + 1) * 1024],
                       V("wost" if fs % 2 == 0 else "wost1"))
                    mark(f"wo0c_{fs}", "v")
            else:
                for fs in range(16):
                    iq = "sy" if fs % 2 == 0 else "s"
                    if emit:
                        tgt, val = marks["wih_free"]
                        if fs < 2:
                            eng[iq].wait_ge(sem[tgt], val)
                            eng["g"].wait_ge(sem[tgt], val)
                        if fs >= 2:
                            tgt, val = marks[f"wi1c_{fs - 2}"]
                            eng[iq].wait_ge(sem[tgt], val)
                            tgt, val = marks[f"wo1c_{fs - 2}"]
                            eng["g"].wait_ge(sem[tgt], val)
                    dma(iq, f"dwq{fs % 2}",
                        V("wist" if fs % 2 == 0 else "wist1"), wi_sb[16 + fs])
                    mark(f"wi1d_{fs}", f"dwq{fs % 2}")
                    wt("v", f"wi1d_{fs}")
                    op("v", "tensor_copy",
                       V("wih")[:, fs * 1024:(fs + 1) * 1024],
                       V("wist" if fs % 2 == 0 else "wist1"))
                    mark(f"wi1c_{fs}", "v")
                    dma("g", f"dtb{fs % 2}", V("wost" if fs % 2 == 0 else "wost1"),
                        wo_sb[(16 + fs) * 128:(17 + fs) * 128])
                    mark(f"wo1d_{fs}", f"dtb{fs % 2}")
                    wt("v", f"wo1d_{fs}")
                    op("v", "tensor_copy",
                       V("woh")[:, fs * 1024:(fs + 1) * 1024],
                       V("wost" if fs % 2 == 0 else "wost1"))
                    mark(f"wo1c_{fs}", "v")
            for qt in range(8):
                t0 = qt * 256
                for fli in range(17):
                    if fli < 16:
                        fl = fli
                        fs = sp * 16 + fl
                        pb = fl % 2
                        if emit:
                            if sp == 0:
                                tgt, val = marks[f"wo0c_{fl}"]
                                eng["t"].wait_ge(sem[tgt], val)
                            if sp == 1:
                                tgt, val = marks[f"wi1c_{fl}"]
                                eng["t"].wait_ge(sem[tgt], val)
                                tgt, val = marks[f"wo1c_{fl}"]
                                eng["t"].wait_ge(sem[tgt], val)
                            if fl >= 2:
                                tgt, val = marks[f"relu_{sp}_{qt}_{fl - 2}"]
                                eng["t"].wait_ge(sem[tgt], val)
                            if fl == 0 and (sp * 8 + qt) >= 1:
                                k = sp * 8 + qt - 1
                                tgt, val = marks[f"fevac_{k // 8}_{k % 8}"]
                                eng["t"].wait_ge(sem[tgt], val)
                        for dc in range(DC):
                            op("t", "matmul", PV(f"pwi{pb}"),
                               V("wih")[:, fl * 1024 + dc * 128: fl * 1024
                                        + (dc + 1) * 128],
                               V("xnT_own")[:, dc * SH + t0: dc * SH + t0
                                            + 256],
                               start=(dc == 0), stop=False, _inc=0)
                        op("t", "matmul", PV(f"pwi{pb}"),
                           V("lbbf")[0:ER, fs * 128:(fs + 1) * 128],
                           V("tp")[0:ER, t0:t0 + 256], start=False, stop=True,
                           _inc=DC + 1)
                        mark(f"wips_{sp}_{qt}_{fl}", "t")
                        wt("s", f"wips_{sp}_{qt}_{fl}")
                        op("s", "activation", V(f"inter{pb}"), PV(f"pwi{pb}"),
                           AF.Relu, bias=V("wib_t")[:, fs:fs + 1])
                        mark(f"relu_{sp}_{qt}_{fl}", "s")
                    if fli >= 1:
                        fl = fli - 1
                        pb = fl % 2
                        wt("t", f"relu_{sp}_{qt}_{fl}")
                        for w in range(2):
                            for j in range(2):
                                op("t", "matmul",
                                   PV(f"ptok{w}")[:, j * 512:(j + 1) * 512],
                                   V(f"inter{pb}")[:, w * 128:(w + 1) * 128],
                                   V("woh")[:, fl * 1024 + j * 512: fl * 1024
                                            + (j + 1) * 512],
                                   start=(fl == 0), stop=(fl == 15),
                                   _inc=(0 if (w, j) != (1, 1) else 4))
                        mark(f"wops_{sp}_{qt}_{fl}", "t")
                mark(f"ffps_{sp}_{qt}", "t")
                wt("v", f"ffps_{sp}_{qt}")
                if emit and (sp * 8 + qt) >= 1:
                    k = sp * 8 + qt - 1
                    for w_ in range(2):
                        tgt, val = marks[f"fsp{w_}_{k // 8}_{k % 8}"]
                        eng["v"].wait_ge(sem[tgt], val)
                if emit and sp == 0 and qt == 0:
                    for t in ("dsv0", "dsv1"):
                        tgt, val = marks[f"spill_total_{t}"]
                        if val > 0:
                            eng["v"].wait_ge(sem[tgt], val)
                for w in range(2):
                    if sp == 0:
                        op("v", "tensor_copy", V(f"ffv{w}"), PV(f"ptok{w}"))
                    else:
                        op("v", "tensor_add", V(f"ffv{w}"), PV(f"ptok{w}"),
                           V("wob_bc"))
                mark(f"fevac_{sp}_{qt}", "v")
                wt("g", f"fevac_{sp}_{qt}")
                tgt_sp = ffn0_sp if sp == 0 else ffn1_sp
                for w in range(2):
                    dma("g", f"dff{w}", tgt_sp[t0 + w * 128:t0 + w * 128
                                               + 128, :], V(f"ffv{w}"))
                    mark(f"fsp{w}_{sp}_{qt}", f"dff{w}")
            if sp == 0:
                mark("wih_free", "t")
        if not emit:
            for t in ("dff0", "dff1"):
                marks[f"bp_ff_{t}"] = (t, cnt[t])
        barrier("Bp")

        def stream_head(h):
            p, side = h // 2, h % 2
            bi = h % 2
            if emit and h >= 2:
                for mk in (f"hcons_t_{h - 2}", f"hcons_v_{h - 2}",
                           f"hcons_g_{h - 2}"):
                    tgt, val = marks[mk]
                    if val > 0:
                        eng["sy"].wait_ge(sem[tgt], val)
                        eng["g"].wait_ge(sem[tgt], val)
            for qr in range(4):
                dma("sy", f"dta{bi}", V(f"t1b{bi}")[qr * 32:(qr + 1) * 32, :],
                    t1_in[h, qr * 32:(qr + 1) * 32, :])
                dma("g", f"dtb{bi}", V(f"t2b{bi}")[qr * 32:(qr + 1) * 32, :],
                    t2_in[h, qr * 32:(qr + 1) * 32, :])
            dma("g", f"dst{bi}", V(f"ktb{bi}")[0:64, :],
                kT_sp[p, side * 64:side * 64 + 64, :])
            dma("g", f"dst{bi}", V(f"veb{bi}"),
                v_sp[p, :, side * 4096:(side + 1) * 4096])
            dma("g", f"dst{bi}", V(f"qtb{bi}")[0:64, :],
                qT_sp[p, side * 64:side * 64 + 64, :])
            mark(f"hs1_{h}", f"dta{bi}")
            mark(f"hs2_{h}", f"dtb{bi}")
            mark(f"hs3_{h}", f"dst{bi}")

        if not emit:
            for t in ("dsp0", "dsp1", "dsv0", "dsv1"):
                marks[f"spill_total_{t}"] = (t, cnt[t])
        else:
            for t in ("dsp0", "dsp1", "dsv0", "dsv1"):
                tgt, val = marks[f"spill_total_{t}"]
                if val > 0:
                    eng["g"].wait_ge(sem[tgt], val)
        stream_head(0)
        stream_head(1)
        for hc in range(DC):
            if emit and hc == 0:
                for t in ("dff0", "dff1"):
                    tgt, val = marks[f"bp_ff_{t}"]
                    if val > 0:
                        eng["s"].wait_ge(sem[tgt], val)
            if emit and hc >= 2:
                tgt, val = marks[f"woc_{hc - 2}"]
                eng["sy"].wait_ge(sem[tgt], val)
            dma("sy", f"dwq{hc % 2}", V(f"wouts{hc % 2}"), wout_sb[hc])
            mark(f"wod_{hc}", f"dwq{hc % 2}")
            wt("s", f"wod_{hc}")
            op("s", "copy", V("wobf2")[:, hc * D:(hc + 1) * D],
               V(f"wouts{hc % 2}"))
            mark(f"woc_{hc}", "s")
        mark("woutc", "s")

        def kept_tiles(h, qtp):
            theta = 80.0 * (2.0 ** (h / 2.0))
            qlo, qhi = qtp * 1024, qtp * 1024 + 1023
            ks = []
            for kc in range(32):
                rlo, rhi = 128 * kc, 128 * kc + 127
                if kc < 16:
                    dmin = max(0, rlo - qhi, qlo - rhi)
                else:
                    d0 = rlo - qhi
                    d1 = qlo + 4096 - rhi
                    dmin = min(d0, d1)
                if dmin <= theta:
                    ks.append(kc)
            return ks

        for h in range(H):
            bi = h % 2
            p, side = h // 2, h % 2
            if emit:
                for key, e in ((f"hs1_{h}", "v"), (f"hs2_{h}", "v"),
                               (f"hs1_{h}", "g"), (f"hs2_{h}", "g"),
                               (f"hs3_{h}", "t")):
                    tgt, val = marks[key]
                    eng[e].wait_ge(sem[tgt], val)
            for qtp in range(2):
                q0 = qtp * 1024
                ks = kept_tiles(h, qtp)
                nk = len(ks)
                pks = (kept_tiles(h, qtp - 1) if qtp else
                       (kept_tiles(h - 1, 1) if h else None))
                for kci in range(nk + 1):
                    if kci < nk:
                        kc = ks[kci]
                        sk = kci % 3
                        if emit:
                            if kci >= 3:
                                tgt, val = marks[
                                    f"exp_{h}_{qtp}_{ks[kci - 3]}"]
                                eng["t"].wait_ge(sem[tgt], val)
                            elif pks is not None:
                                ph, pq = (h, qtp - 1) if qtp else (h - 1, 1)
                                tgt, val = marks[
                                    f"exp_{ph}_{pq}_{pks[kci - 3]}"]
                                eng["t"].wait_ge(sem[tgt], val)
                                if kci == 0:
                                    tgt, val = marks[f"anorm_{ph}_{pq}"]
                                    eng["t"].wait_ge(sem[tgt], val)
                        for j in range(2):
                            op("t", "matmul",
                               PV(f"pss{sk}")[:, j * 512:(j + 1) * 512],
                               V(f"ktb{bi}")[0:64, kc * 128:(kc + 1) * 128],
                               V(f"qtb{bi}")[0:64, q0 + j * 512: q0 + (j + 1)
                                             * 512],
                               start=True, stop=True)
                        mark(f"sps_{h}_{qtp}_{kc}", "t")
                        wt("v", f"sps_{h}_{qtp}_{kc}")
                        tb = V(f"t1b{bi}") if kc < 16 else V(f"t2b{bi}")
                        jj0 = TC0 + q0 - 128 * (kc % 16)
                        op("v", "tensor_tensor", PV(f"pss{sk}"),
                           PV(f"pss{sk}"), tb[:, jj0:jj0 + 1024], ALU.add)
                        mark(f"badd_{h}_{qtp}_{kc}", "v")
                        wt("s", f"badd_{h}_{qtp}_{kc}")
                        if emit and kci >= 3:
                            tgt, val = marks[f"pv_{h}_{qtp}_{ks[kci - 3]}"]
                            eng["s"].wait_ge(sem[tgt], val)
                        op("s", "activation", V(f"esb{kci % 3}"),
                           PV(f"pss{sk}"), AF.Exp)
                        mark(f"exp_{h}_{qtp}_{kc}", "s")
                    if kci >= 1:
                        kc = ks[kci - 1]
                        wt("t", f"exp_{h}_{qtp}_{kc}")
                        for j in range(2):
                            op("t", "matmul", PV(f"pa{j}"),
                               V(f"veb{bi}")[:, kc * 128:(kc + 1) * 128],
                               V(f"esb{(kci - 1) % 3}")[:, j * 512:(j + 1)
                                                        * 512],
                               start=(kci == 1), stop=(kci == nk))
                        mark(f"pv_{h}_{qtp}_{kc}", "t")
                mark(f"pvdone_{h}_{qtp}", "t")
                wt("v", f"pvdone_{h}_{qtp}")
                for j in range(2):
                    op("v", "reciprocal", V("rden")[0:1, j * 512:(j + 1)
                                                    * 512],
                       PV(f"pa{j}")[64:65, :])
                mark(f"rden_{h}_{qtp}", "v")
                wt("g", f"rden_{h}_{qtp}")
                if emit and (h + qtp) > 0:
                    ph, pq = (h, qtp - 1) if qtp else (h - 1, 1)
                    tgt, val = marks[f"bcr_{ph}_{pq}"]
                    eng["g"].wait_ge(sem[tgt], val)
                dma("g", "dbc", bc_scr, V("rden")[0:1, :])
                mark(f"bcw_{h}_{qtp}", "dbc")
                wt("g", f"bcw_{h}_{qtp}")
                dma("g", "dbc", V("bcb"),
                    bass.AP(bc_scr.tensor, 0, [[0, 64], [1, 1024]]))
                mark(f"bcr_{h}_{qtp}", "dbc")
                wt("v", f"bcr_{h}_{qtp}")
                for j in range(2):
                    op("v", "tensor_tensor",
                       V("aoT")[side * 64:side * 64 + 64,
                                p * SH + q0 + j * 512: p * SH + q0 + (j + 1)
                                * 512],
                       PV(f"pa{j}")[0:64, :],
                       V("bcb")[0:64, j * 512:(j + 1) * 512], ALU.mult)
                mark(f"anorm_{h}_{qtp}", "v")
            mark(f"hcons_t_{h}", "t")
            mark(f"hcons_v_{h}", "v")
            mark(f"hcons_g_{h}", "g")
            if h + 2 < H:
                stream_head(h + 2)
        if emit:
            for key, e in (("woutc", "t"), (f"anorm_{H - 1}_1", "t"),
                           (f"exp_{H - 1}_1_31", "t")):
                tgt, val = marks[key]
                eng[e].wait_ge(sem[tgt], val)
        for qt in range(16):
            qb = qt % 2
            t0 = qt * 128
            if emit and qt >= 2:
                tgt, val = marks[f"yevac_{qt - 2}"]
                eng["t"].wait_ge(sem[tgt], val)
            for hc in range(DC):
                for j in range(2):
                    op("t", "matmul",
                       PV(f"pty{qb}")[:, j * 512:(j + 1) * 512],
                       V("aoT")[:, hc * SH + t0: hc * SH + t0 + 128],
                       V("wobf2")[:, hc * D + j * 512: hc * D + (j + 1)
                                  * 512],
                       start=(hc == 0), stop=(hc == DC - 1))
            mark(f"yps_{qt}", "t")
            wt("v", f"yps_{qt}")
            if emit and qt >= 2:
                tgt, val = marks[f"ysp_{qt - 2}"]
                eng["v"].wait_ge(sem[tgt], val)
            op("v", "tensor_add", V(f"ysv{qb}"), PV(f"pty{qb}"),
               V("bout_bc"))
            mark(f"yevac_{qt}", "v")
            wt("g", f"yevac_{qt}")
            dma("g", f"dff{qb}", aty_sp[t0:t0 + 128, :], V(f"ysv{qb}"))
            mark(f"ysp_{qt}", f"dff{qb}")
        barrier("C")

        if not emit:
            for t in ("dff0", "dff1"):
                marks[f"ff_total_{t}"] = (t, cnt[t])
        else:
            for t in ("dff0", "dff1"):
                tgt, val = marks[f"ff_total_{t}"]
                if val > 0:
                    eng["sy"].wait_ge(sem[tgt], val)
        for tc in range(16):
            bi = tc % 2
            if emit and tc >= 2:
                tgt, val = marks[f"osum_{tc - 2}"]
                eng["sy"].wait_ge(sem[tgt], val)
            r0 = tc * 128
            dma("sy", f"dx{bi}", V(f"ya{bi}"), aty_sp[r0:r0 + 128, :])
            dma("sy", f"dx{bi}", V(f"fa{bi}"), ffn0_sp[r0:r0 + 128, :])
            dma("sy", f"dx{bi}", V(f"fb{bi}"), ffn1_sp[r0:r0 + 128, :])
            dma("sy", f"dx{bi}", V(f"xra{bi}"), x_in[r0:r0 + 128, :])
            mark(f"ein_{tc}", f"dx{bi}")
            wt("v", f"ein_{tc}")
            if emit and tc >= 2:
                tgt, val = marks[f"out_{tc - 2}"]
                eng["v"].wait_ge(sem[tgt], val)
            op("v", "tensor_add", V(f"sm{bi}"), V(f"ya{bi}"), V(f"fa{bi}"))
            op("v", "tensor_add", V(f"sm{bi}"), V(f"sm{bi}"), V(f"fb{bi}"))
            op("v", "tensor_add", V(f"ob{bi}"), V(f"sm{bi}"), V(f"xra{bi}"))
            mark(f"osum_{tc}", "v")
            wt("g", f"osum_{tc}")
            dma("g", f"do{bi}", out[r0:r0 + 128, :], V(f"ob{bi}"))
            mark(f"out_{tc}", f"do{bi}")
        if emit:
            eng["g"].wait_ge(sem["do0"], cnt["do0"])
            eng["g"].wait_ge(sem["do1"], cnt["do1"])

    L = {}

    def place(name, off, part, ncols, dtype, prow=0):
        L[name] = (off, part, ncols, dtype, prow)

    place("idbf", 0, 128, 128, BF16)
    place("idf32", 256, 128, 128, F32)
    place("ecols", 768, 128, ER, F32)
    place("bq_t", 896, 128, DC, F32)
    place("bk_t", 928, 128, DC, F32)
    place("bqs_t", 960, 128, DC, F32)
    place("wib_t", 1024, 128, 32, F32)
    place("gb_t", 1152, E, 1, F32)
    place("ones1", 1216, 1, 128, BF16)
    place("bvf", 1536, 1, D, F32)
    place("bvbf", 5632, 1, D, BF16)
    place("gwf", 7680, 128, DC * E, F32)
    place("gwbf", 7936, 128, DC * E, BF16)
    place("laf", 8192, 128, DC * ER, F32)
    place("labf", 9216, 128, DC * ER, BF16)
    place("lbbf", 9728, ER, F, BF16)
    place("wob_bc", 26112, 128, D, F32)
    place("bout_bc", 30208, 128, D, F32)
    place("xnT_own", 34304, 128, DC * SH, BF16)
    place("xnT_oth", 67072, 128, DC * SH, BF16)
    place("xa0", 99840, 128, D, F32)
    place("xa1", 103936, 128, D, F32)
    place("xc", 108032, 128, D, F32)
    place("sq", 112128, 128, D, F32)
    place("xnc0", 116224, 128, D, BF16)
    place("xnc1", 118272, 128, D, BF16)
    place("st1", 120320, 128, 8, F32)
    place("lbf", 120576, ER, F, F32)
    place("wf320", 99840, 128, 3 * D, F32)
    place("wf321", 112128, 128, 3 * D, F32)
    place("wbf0", 124416, 128, 3 * D, BF16)
    place("wbf1", 130560, 128, 3 * D, BF16)
    place("ksp0", 136704, 128, 512, BF16)
    place("ksp1", 137728, 128, 512, BF16)
    place("qsp0", 138752, 128, 512, BF16)
    place("qsp1", 139776, 128, 512, BF16)
    place("vsp0", 140800, 128, 4096, BF16)
    place("vsp1", 185984, 128, 4096, BF16)
    place("wih", 149120, 128, 16 * 1024, BF16)
    place("wistp", 181888, 128, D, F32)
    place("wostp", 185984, 128, D, F32)
    place("woh", 99840, 128, 16 * 1024, BF16)
    place("wist", 132608, 128, D, F32)
    place("wist1", 136704, 128, D, F32)
    place("wost", 140800, 128, D, F32)
    place("wost1", 144896, 128, D, F32)
    place("inter0", 181888, 128, 256, BF16)
    place("inter1", 17920, 128, 256, BF16)
    place("ffv0", 182400, 128, D, F32)
    place("ffv1", 186496, 128, D, F32)
    place("glog", 67072, E, SH, F32)
    place("gmx", 75264, 128, E, F32)
    place("gmi", 75296, 128, E, U32)
    place("gif", 75328, 128, 1, F32)
    place("mtok", 75392, 128, ER, BF16)
    place("maskT", 75520, ER, SH, BF16)
    place("tp", 79616, ER, SH, BF16)
    place("t1b0", 34304, 128, TW, F32)
    place("t1b1", 50176, 128, TW, F32)
    place("t2b0", 66048, 128, TW, F32)
    place("t2b1", 81920, 128, TW, F32)
    place("ktb0", 97792, 128, S, BF16)
    place("ktb1", 105984, 128, S, BF16)
    place("veb0", 114176, 128, 4096, BF16)
    place("veb1", 122368, 128, 4096, BF16)
    place("qtb0", 130560, 128, SH, BF16)
    place("qtb1", 134656, 128, SH, BF16)
    place("esb0", 138752, 128, 1024, BF16)
    place("esb1", 140800, 128, 1024, BF16)
    place("esb2", 192128, 128, 1024, BF16)
    place("rden", 97792, 1, 1024, F32, prow=64)
    place("bcb", 105984, 64, 1024, F32, prow=64)
    place("aoT", 142976, 128, DC * SH, BF16)
    place("wobf2", 175744, 128, DC * D, BF16)
    place("wouts0", 17920, 128, D, F32)
    place("wouts1", 22016, 128, D, F32)
    place("ysv0", 130560, 128, D, F32)
    place("ysv1", 134656, 128, D, F32)
    place("ya0", 34304, 128, D, F32)
    place("ya1", 38400, 128, D, F32)
    place("fa0", 42496, 128, D, F32)
    place("fa1", 46592, 128, D, F32)
    place("fb0", 50688, 128, D, F32)
    place("fb1", 54784, 128, D, F32)
    place("xra0", 58880, 128, D, F32)
    place("xra1", 62976, 128, D, F32)
    place("sm0", 67072, 128, D, F32)
    place("sm1", 71168, 128, D, F32)
    place("ob0", 75264, 128, D, F32)
    place("ob1", 79360, 128, D, F32)

    PL = {
        "tr0": (0, 128, 1024, BF16), "tr1": (512, 128, 1024, BF16),
        "pk0": (1024, 128, 512, F32), "pk1": (1536, 128, 512, F32),
        "pq0": (2048, 128, 512, F32), "pq1": (2560, 128, 512, F32),
        "pv0": (3072, 128, 128, F32), "pv1": (3584, 128, 128, F32),
        "pg0": (2048, E, 512, F32), "pg1": (2560, E, 512, F32),
        "pgt0": (3072, 128, E, F32), "pgt1": (3080, 128, E, F32),
        "pmt0": (3088, ER, 128, BF16), "pmt1": (3152, ER, 128, BF16),
        "pt320": (2048, ER, 512, F32), "pt321": (2560, ER, 512, F32),
        "pwi0": (3072, 128, 256, F32), "pwi1": (3584, 128, 256, F32),
        "ptok0": (0, 128, 1024, F32), "ptok1": (1024, 128, 1024, F32),
        "pss0": (0, 128, 1024, F32), "pss1": (1024, 128, 1024, F32),
        "pss2": (2048, 128, 1024, F32),
        "pa0": (3072, 128, 512, F32), "pa1": (3584, 128, 512, F32),
        "pty0": (0, 128, 1024, F32), "pty1": (1024, 128, 1024, F32),
    }

    es = ExitStack()
    with es:
        sbA = es.enter_context(nc.sbuf_tensor("sbA", [128, ARENA // 4], F32))
        psA = es.enter_context(nc.psum_tensor("psA", [128, 4096], F32))
        sems = {}
        for sname in CH + DM:
            sems[sname] = es.enter_context(nc.semaphore(f"sem_{sname}"))
        block = es.enter_context(nc.Block())

        def V(name):
            off, part, ncols, dtype, prow = L[name]
            esz = 2 if dtype == BF16 else 4
            assert off % 4 == 0 and off + ncols * esz <= ARENA, name
            v = sbA[prow:prow + part, off // 4: (off + ncols * esz + 3) // 4]
            if dtype != F32:
                v = v.bitcast(dtype)
            if v.shape[1] != ncols:
                v = v[:, 0:ncols]
            return v

        def PV(name):
            off, part, ncols, dtype = PL[name]
            esz = 2 if dtype == BF16 else 4
            v = psA[0:part, off: off + (ncols * esz + 3) // 4]
            if dtype != F32:
                v = v.bitcast(dtype)
            if v.shape[1] != ncols:
                v = v[:, 0:ncols]
            return v

        cx = {"emit": False, "cnt": {k: 0 for k in CH + DM},
              "eng": {k: None for k in CH + ["sy"]}, "sem": sems,
              "V": V, "PV": PV}
        prog(cx)

        class _DummyInstr:
            def then_inc(self, *a, **kw):
                return self

        class _DummyEng:
            def wait_ge(self, *a, **kw):
                return None

            def __getattr__(self, name):
                return lambda *a, **kw: _DummyInstr()

        def emit_for(engname, real):
            eng = {k: _DummyEng() for k in CH + ["sy"]}
            eng[engname] = real
            cx2 = {"emit": True, "cnt": {k: 0 for k in CH + DM},
                   "eng": eng, "sem": sems, "V": V, "PV": PV}
            prog(cx2)

        @block.tensor
        def _(t):
            emit_for("t", t)

        @block.vector
        def _(v):
            emit_for("v", v)

        @block.scalar
        def _(s):
            emit_for("s", s)

        @block.gpsimd
        def _(g):
            emit_for("g", g)

        @block.sync
        def _(sy):
            emit_for("sy", sy)

    return nc


_cache = {}


def kernel(**inputs):
    f32 = np.float32
    xf = np.ascontiguousarray(inputs["L0_attn_x"], dtype=f32)
    wqkv = np.ascontiguousarray(inputs["L0_attn_W_qkv"], dtype=f32)
    bqkv = np.ascontiguousarray(inputs["L0_attn_b_qkv"], dtype=f32)
    wout = np.ascontiguousarray(inputs["L0_attn_W_out"], dtype=f32)
    bout = np.ascontiguousarray(inputs["L0_attn_b_out"], dtype=f32)
    gw = np.ascontiguousarray(inputs["L0_ffn_gate_W"], dtype=f32)
    gb = np.ascontiguousarray(inputs["L0_ffn_gate_b"], dtype=f32)
    wi = np.ascontiguousarray(inputs["L0_ffn_wi_W"], dtype=f32)
    wib = np.ascontiguousarray(inputs["L0_ffn_wi_b"], dtype=f32)
    wo = np.ascontiguousarray(inputs["L0_ffn_wo_W"], dtype=f32)
    wob = np.ascontiguousarray(inputs["L0_ffn_wo_b"], dtype=f32)
    la = np.ascontiguousarray(inputs["L0_ffn_lora_A"], dtype=f32)
    lb = np.ascontiguousarray(inputs["L0_ffn_lora_B"], dtype=f32)

    slopes = np.power(np.float32(2.0),
                      (8.0 * np.arange(0, -H, -1, dtype=f32) / H).astype(f32))
    kk = np.arange(128)
    jj = np.arange(TW)
    base = (jj[None, :] - TC0) - kk[:, None]
    T1 = np.empty((H, 128, TW), dtype=f32)
    T2h = {0: np.empty((H, 128, TW), dtype=f32),
           1: np.empty((H, 128, TW), dtype=f32)}
    SHIFT = np.float32(64.0)
    for h in range(H):
        s = slopes[h]
        T1[h] = (-s) * np.abs(base).astype(f32) - SHIFT
        T2h[0][h] = (-s) * np.abs(base - 2048).astype(f32) - SHIFT
        T2h[1][h] = (-s) * np.abs(base + 2048).astype(f32) - SHIFT

    def pairize(w):
        return np.ascontiguousarray(
            w.reshape(8, 128, 8, 128).transpose(0, 3, 2, 1).reshape(
                8, 128, 1024))

    wq_p = pairize(wqkv[0:D] )
    wk_p = pairize(wqkv[D:2 * D])
    wv_p = pairize(wqkv[2 * D:3 * D])
    wqkv_sb = np.ascontiguousarray(
        np.concatenate([wq_p, wk_p, wv_p], axis=2))
    shared = {
        "wqkv_sb": wqkv_sb,
        "bq_sb": np.ascontiguousarray(bqkv[0:D].reshape(8, 128).T),
        "bk_sb": np.ascontiguousarray(bqkv[D:2 * D].reshape(8, 128).T),
        "bv_row": np.ascontiguousarray(bqkv[2 * D:3 * D][None, :]),
        "wout_sb": np.ascontiguousarray(wout.T.reshape(8, 128, D)),
        "boutbc": np.ascontiguousarray(
            np.broadcast_to(bout[None, :], (128, D))),
        "wi_sb": np.ascontiguousarray(
            wi.reshape(32, 128, 8, 128).transpose(0, 3, 2, 1).reshape(
                32, 128, D)),
        "wib_sb": np.ascontiguousarray(wib.reshape(32, 128).T),
        "wo_sb": np.ascontiguousarray(wo.T),
        "wobbc": np.ascontiguousarray(
            np.broadcast_to(wob[None, :], (128, D))),
        "gw_sb": np.ascontiguousarray(
            gw.T.reshape(8, 128, E).transpose(1, 0, 2).reshape(128, 8 * E)),
        "gb_sb": np.ascontiguousarray(gb[:, None]),
        "la_sb": np.ascontiguousarray(
            la.reshape(ER, 8, 128).transpose(2, 1, 0).reshape(128, 8 * ER)),
        "lb_sb": np.ascontiguousarray(lb.transpose(0, 2, 1).reshape(ER, F)),
        "T1": T1,
        "idbf": np.eye(128, dtype=ml_dtypes.bfloat16),
        "idf32": np.eye(128, dtype=f32),
        "ecols": np.ascontiguousarray(np.broadcast_to(
            np.repeat(np.arange(E, dtype=f32), R)[None, :], (128, ER))),
    }

    if "nc" not in _cache:
        _cache["nc"] = build_program()
    nc = _cache["nc"]

    in_maps = []
    for core in range(8):
        b, half = core // 2, core % 2
        m = dict(shared)
        m["x"] = np.ascontiguousarray(np.roll(xf[b], -half * SH, axis=0))
        m["T2"] = T2h[half]
        in_maps.append(m)

    res = run_bass_kernel_spmd(nc, in_maps, core_ids=list(range(8)))
    outf = np.empty((B, S, D), dtype=np.float32)
    for core in range(8):
        b, half = core // 2, core % 2
        outf[b, half * SH:(half + 1) * SH] = res.results[core]["out"]
    return outf
